# version 03 (final)
# baseline (speedup 1.0000x reference)
import numpy as np
import ml_dtypes
from contextlib import ExitStack

import concourse.bass as bass
import concourse.mybir as mybir
from concourse.bass_utils import run_bass_kernel_spmd

F32 = mybir.dt.float32
BF16 = mybir.dt.bfloat16
U8 = mybir.dt.uint8
AF = mybir.ActivationFunctionType
ALU = mybir.AluOpType
AX = mybir.AxisListType

NCORES = 8
T = 2048
D = 1024
SL = 512
NSL = T // SL
NDC = D // 128
CH = 128
NCH = T // CH
DH = 4096
DI = 2048
DS = 64
NH = 8

N_BLOCKS = 4
MSTAGE = 5
MSUB = 6
DEBUG_TAPS = 0


class Sem:
    def __init__(self, nc, ctx, name):
        self.h = ctx.enter_context(nc.semaphore(name))
        self.n = 0

    def inc(self, inst, k=1):
        inst.then_inc(self.h, k)
        self.n += k
        return self.n


class MK:
    def __init__(self):
        self.nc = bass.Bass("TRN2", target_bir_lowering=False)
        self.inp = {}
        self.out = {}
        self.dbg = {}

    def din(self, name, shape, dt=F32):
        ap = self.nc.dram_tensor(name, list(shape), dt, kind="ExternalInput").ap()
        self.inp[name] = ap
        return ap

    def dout(self, name, shape, dt=F32):
        ap = self.nc.dram_tensor(name, list(shape), dt, kind="ExternalOutput").ap()
        self.out[name] = ap
        return ap

    def sb(self, name, shape, dt=F32, stack=None):
        return (stack or self.ctx).enter_context(self.nc.sbuf_tensor(name, list(shape), dt))

    def ps(self, name, shape, dt=F32, stack=None):
        return (stack or self.ctx).enter_context(self.nc.psum_tensor(name, list(shape), dt))

    def build(self):
        nc = self.nc
        xT = self.din("xT", [D, T])
        xo = self.dout("xo", [D, T])
        self.declare_weights()

        self.ctx = ExitStack()
        with self.ctx:
            ctx = self.ctx
            self.dS = Sem(nc, ctx, "dS")
            self.dG = Sem(nc, ctx, "dG")
            self.sT = Sem(nc, ctx, "sT")
            self.sV = Sem(nc, ctx, "sV")
            self.sS = Sem(nc, ctx, "sS")
            self.sG = Sem(nc, ctx, "sG")
            self.sC = Sem(nc, ctx, "sC")
            self.scr_free_v = 0
            self.scr_free_t = 0
            self.pstat_free_v = 0
            self.rowtmp_free_s = 0
            self.rowuse_t = 0
            self.pb_free_v = 0

            self.X = self.sb("X", [128, NDC, T], BF16)
            self.IDENT = self.sb("IDENT", [128, 128], BF16)
            self.ONEC_BF = self.sb("ONEC_BF", [128, 1], BF16)
            self.ONER_BF = self.sb("ONER_BF", [1, SL], BF16)
            self.ONER128_F = self.sb("ONER128_F", [1, 128], F32)
            self.ROWS = self.sb("ROWS", [1, 4 * SL], F32)
            self.SCR_BF2 = self.sb("SCR_BF2", [128, NDC, SL], BF16)
            self.SCR_F1 = self.sb("SCR_F1", [128, SL], F32)
            self.EPS = self.sb("EPS", [1, 4], F32)

            self.block = ctx.enter_context(nc.Block())
            blk = self.block

            @blk.gpsimd
            def _(g):
                for c in range(NDC):
                    self.dG.inc(g.dma_start(
                        out=self.X[:, c, :],
                        in_=xT[c * 128:(c + 1) * 128, :]), 16)
                self.dG.inc(g.dma_start(out=self.IDENT[:], in_=self.inp["identf"][:, :]), 16)
            x_loaded = self.dG.n
            ident_ld = self.dG.n

            @blk.vector
            def _(v):
                v.memset(self.ONEC_BF[:], 1.0)
                v.memset(self.ONER_BF[:], 1.0)
                v.memset(self.ONER128_F[:], 1.0)
                v.memset(self.EPS[0:1, 0:1], 1e-5)
                v.memset(self.EPS[0:1, 1:2], 1e-6)
                v.memset(self.EPS[0:1, 2:3], 1.0)
                self.sV.inc(v.memset(self.EPS[0:1, 3:4], 0.0))
            consts_done = self.sV.n

            self.marks = dict(x_loaded=x_loaded, ident_ld=ident_ld, consts=consts_done)

            if N_BLOCKS >= 1:
                self.phase_attn()
            if N_BLOCKS >= 2:
                self.phase_ffn()
            if N_BLOCKS >= 3:
                self.phase_mamba()
            if N_BLOCKS >= 4:
                self.phase_l1ffn()

            stream_ready = dict(T=self.sT.n, V=self.sV.n, S=self.sS.n, G=self.sG.n)

            @blk.gpsimd
            def _(g):
                g.wait_ge(self.sT.h, stream_ready["T"])
                g.wait_ge(self.sV.h, stream_ready["V"])
                g.wait_ge(self.sS.h, stream_ready["S"])
                for c in range(NDC):
                    self.dG.inc(g.dma_start(
                        out=xo[c * 128:(c + 1) * 128, :],
                        in_=self.X[:, c, :]), 16)
                for name, (ap_fn, shape, dt) in getattr(self, "deferred_taps", []):
                    dten = self.dout(name, shape, dt)
                    self.dG.inc(g.dma_start(out=dten[:, :], in_=ap_fn()), 16)
                g.wait_ge(self.dG.h, self.dG.n)

        return nc

    def declare_weights(self):
        d = self.din
        d("identf", [128, 128])
        d("wqkv", [D, 1536]); d("bqkv", [1, 1536])
        d("wfc", [512, D]); d("bfc", [1, D])
        d("gamcol", [D, 1]); d("betcol", [D, 1])
        d("w1T", [D, DH]); d("w2T", [D, DH])
        d("h1T", [DH, DH]); d("h2T", [DH, DH])
        d("w3T", [DH, D])
        d("s1", [128, 32]); d("s2", [128, 32]); d("hs1", [128, 32]); d("hs2", [128, 32])
        d("s3", [128, 8])
        d("winT", [D, 2 * DI])
        d("convw", [128, 64])
        d("convb", [128, 16])
        d("wxT", [DI, 129])
        d("alog", [DS, 1]); d("dvec", [128, 16])
        d("woutT", [DI, D])
        d("l1wT", [D, D])
        d("lvlmask", [128, 8 * 128])
        d("gdif", [128, 8 * 128])
        d("tile8", [128, 8 * 128])
        d("ut", [128, 128])
        d("sel127", [128, 1])
        d("henddif", [128, 128])
        d("cmask", [DS, 8])
        d("hmask", [128, 8])


    def emit_taps(self):
        if not hasattr(self, "deferred_taps"):
            self.deferred_taps = []
        self.deferred_taps.extend(self.dbg.items())
        self.dbg = {}

    def rows(self, slot, n=SL):
        return self.ROWS[0:1, slot * SL:slot * SL + n]

    def ln_stats_plain(self, src_fn, eps_ap, r_slot, m2_slot, psta, pstb,
                       extra_wait=None, src_is_bf=False):
        blk = self.block

        @blk.scalar
        def _(s):
            if extra_wait is not None:
                extra_wait(s)
            s.wait_ge(self.sT.h, self.scr_free_t)
            for c in range(NDC):
                ins = s.activation(self.SCR_BF2[:, c, :], src_fn(c), AF.Square)
            self.sS.inc(ins)
        sq_done = self.sS.n

        assert src_is_bf, "only bf16 LN sources supported"
        cast_done = self.sV.n
        bf_fn = src_fn

        @blk.tensor
        def _(t):
            t.wait_ge(self.sS.h, sq_done)
            t.wait_ge(self.sV.h, cast_done)
            t.wait_ge(self.sV.h, self.pstat_free_v)
            for c in range(NDC):
                t.matmul(psta[0:1, 0:SL], self.ONEC_BF[:], bf_fn(c),
                         start=(c == 0), stop=(c == NDC - 1))
            for c in range(NDC):
                ins = t.matmul(pstb[0:1, 0:SL], self.ONEC_BF[:], self.SCR_BF2[:, c, :],
                               start=(c == 0), stop=(c == NDC - 1))
            self.sT.inc(ins)
        stats_done = self.sT.n
        self.scr_free_t = stats_done

        MU = self.rows(2); VAR = self.rows(3)
        R = self.rows(r_slot); M2 = self.rows(m2_slot)
        LNV = R

        @blk.vector
        def _(v):
            v.wait_ge(self.sT.h, stats_done)
            v.wait_ge(self.sT.h, self.rowuse_t)
            v.wait_ge(self.sS.h, self.rowtmp_free_s)
            v.tensor_scalar_mul(MU, psta[0:1, 0:SL], 1.0 / D)
            v.tensor_scalar_mul(VAR, pstb[0:1, 0:SL], 1.0 / D)
            v.tensor_tensor(LNV, MU, MU, ALU.mult)
            ins = v.tensor_tensor(VAR, VAR, LNV, ALU.subtract)
            self.sV.inc(ins)
        var_done = self.sV.n
        self.pstat_free_v = var_done

        @blk.scalar
        def _(s):
            s.wait_ge(self.sV.h, var_done)
            s.activation(LNV, VAR, AF.Ln, bias=eps_ap)
            ins = s.activation(R, LNV, AF.Exp, scale=-0.5)
            self.sS.inc(ins)
        r_done = self.sS.n
        self.rowtmp_free_s = r_done

        @blk.vector
        def _(v):
            v.wait_ge(self.sS.h, r_done)
            v.tensor_tensor(M2, MU, R, ALU.mult)
            ins = v.tensor_scalar_mul(M2, M2, -1.0)
            self.sV.inc(ins)
        rows_done = self.sV.n
        return rows_done

    def bcast2(self, rows_done, r_slot, m2_slot, pb1, pb2, consumed_v_mark):
        blk = self.block

        @blk.tensor
        def _(t):
            t.wait_ge(self.sV.h, rows_done)
            t.wait_ge(self.sV.h, consumed_v_mark)
            t.matmul(pb1[:, 0:SL], self.ONER128_F[:], self.rows(r_slot), start=True, stop=True)
            ins = t.matmul(pb2[:, 0:SL], self.ONER128_F[:], self.rows(m2_slot), start=True, stop=True)
            self.sT.inc(ins)
        self.rowuse_t = self.sT.n
        return self.sT.n

    def ln_apply_to_bf(self, src_fn, dst_fn, bcast_mark, pb1, pb2):
        blk = self.block

        @blk.vector
        def _(v):
            v.wait_ge(self.sT.h, bcast_mark)
            for c in range(NDC):
                v.tensor_tensor(self.SCR_F1[:, 0:SL], src_fn(c), pb1[:, 0:SL], ALU.mult)
                ins = v.tensor_tensor(dst_fn(c), self.SCR_F1[:, 0:SL], pb2[:, 0:SL], ALU.add)
            self.sV.inc(ins)
        return self.sV.n

    def ln_apply_residual(self, src_fn, x_sl, bcast_mark, pb1, pb2):
        blk = self.block

        @blk.vector
        def _(v):
            v.wait_ge(self.sT.h, bcast_mark)
            for c in range(NDC):
                v.tensor_tensor(self.SCR_F1[:, 0:SL], src_fn(c), pb1[:, 0:SL], ALU.mult)
                v.tensor_tensor(self.SCR_F1[:, 0:SL], self.SCR_F1[:, 0:SL], pb2[:, 0:SL], ALU.add)
                ins = v.tensor_tensor(self.X[:, c, x_sl], self.X[:, c, x_sl],
                                      self.SCR_F1[:, 0:SL], ALU.add)
            self.sV.inc(ins)
        return self.sV.n

    def ln_plain(self, src_fn, dst_fn, eps_ap, psta, pstb, pb1, pb2,
                 extra_wait=None, src_is_bf=False):
        rows_done = self.ln_stats_plain(src_fn, eps_ap, 0, 1, psta, pstb,
                                        extra_wait=extra_wait, src_is_bf=src_is_bf)
        bm = self.bcast2(rows_done, 0, 1, pb1, pb2, self.pb_free_v)
        self.pb_bcast_t = bm
        ap = self.ln_apply_to_bf(src_fn, dst_fn, bm, pb1, pb2)
        self.pb_free_v = ap
        return ap

    def ln_residual(self, src_fn, x_sl, eps_ap, psta, pstb, pb1, pb2,
                    extra_wait=None, src_is_bf=False):
        rows_done = self.ln_stats_plain(src_fn, eps_ap, 0, 1, psta, pstb,
                                        extra_wait=extra_wait, src_is_bf=src_is_bf)
        bm = self.bcast2(rows_done, 0, 1, pb1, pb2, self.pb_free_v)
        self.pb_bcast_t = bm
        ap = self.ln_apply_residual(src_fn, x_sl, bm, pb1, pb2)
        self.pb_free_v = ap
        return ap


    def phase_attn(self):
        blk = self.block
        nc = self.nc
        st = ExitStack()
        with st:
            Y0 = self.sb("Y0", [128, NDC, T], BF16, st)
            VBUF = self.sb("VBUF", [128, 4, T], BF16, st)
            QC = self.sb("QC", [128, 2, 512], F32, st)
            KC = self.sb("KC", [128, 2, 512], F32, st)
            SCF = self.sb("SCF", [64, 512], F32, st)
            SCP = self.sb("SCP", [64, 512], F32, st)
            SC_ACC = self.sb("SC_ACC", [64, 512], F32, st)
            ATTF = self.sb("ATTF", [64, 512], F32, st)
            ATT = self.sb("ATT", [128, 512], BF16, st)
            MAXC = self.sb("MAXC", [64, 8], F32, st)
            SUMC = self.sb("SUMC", [64, 8], F32, st)
            RSUM = self.sb("RSUM", [64, 8], F32, st)
            OB = self.sb("OB", [128, 4, SL], BF16, st)
            UB = self.sb("UB", [128, NDC, SL], BF16, st)
            SCR_F2 = self.sb("SCR_F2", [128, SL], F32, st)
            GAMC = self.sb("GAMC", [128, NDC], F32, st)
            BETC = self.sb("BETC", [128, NDC], F32, st)

            P0 = self.ps("P0", [128, 512], F32, st)
            P1 = self.ps("P1", [128, 512], F32, st)
            P2 = self.ps("P2", [128, 512], F32, st)
            P3 = self.ps("P3", [128, 512], F32, st)
            PS_SC = self.ps("PS_SC", [64, 512], F32, st)
            PSTA = self.ps("PSTA", [1, 512], F32, st)
            PSTB = self.ps("PSTB", [1, 512], F32, st)

            st_qkv = ExitStack()
            WQK = self.sb("WQK", [128, NDC, 1024], BF16, st_qkv)
            WQKLO = self.sb("WQKLO", [128, NDC, 1024], BF16, st_qkv)
            WV = self.sb("WV", [128, NDC, 512], BF16, st_qkv)
            BQKV = self.sb("BQKV", [1, 1536], BF16, st_qkv)
            st_wqkf = ExitStack()
            WQKF = self.sb("WQKF", [128, NDC, 512], F32, st_wqkf)

            ccin = nc.dram_tensor("attn_ccin", [64, 512], F32, kind="Internal")
            ccout = nc.dram_tensor("attn_ccout", [64, 512], F32, kind="Internal")

            @blk.gpsimd
            def _(g):
                for c in range(NDC):
                    self.dG.inc(g.dma_start(out=WQK[:, c, :],
                                            in_=self.inp["wqkv"][c * 128:(c + 1) * 128, 0:1024]), 16)
                    self.dG.inc(g.dma_start(out=WV[:, c, :],
                                            in_=self.inp["wqkv"][c * 128:(c + 1) * 128, 1024:1536]), 16)
                self.dG.inc(g.dma_start(out=BQKV[:], in_=self.inp["bqkv"][:, :]), 16)
            wqk_ld = self.dG.n

            for half in range(2):
                @blk.sync
                def _(q, half=half):
                    if half == 1:
                        q.wait_ge(self.sV.h, self.wqklo_mark)
                    for c in range(NDC):
                        self.dS.inc(q.dma_start(
                            out=WQKF[:, c, :],
                            in_=self.inp["wqkv"][c * 128:(c + 1) * 128,
                                                 half * 512:(half + 1) * 512]), 16)
                wqkf_ld = self.dS.n

                @blk.vector
                def _(v, half=half, wqkf_ld=wqkf_ld):
                    v.wait_ge(self.dS.h, wqkf_ld)
                    v.wait_ge(self.dG.h, wqk_ld)
                    for c in range(NDC):
                        ins = v.tensor_tensor(
                            WQKLO[:, c, half * 512:(half + 1) * 512],
                            WQKF[:, c, :],
                            WQK[:, c, half * 512:(half + 1) * 512],
                            ALU.subtract)
                    self.sV.inc(ins)
                self.wqklo_mark = self.sV.n
            wqklo_done = self.sV.n
            st_wqkf.close()

            @blk.sync
            def _(q):
                for c in range(NDC):
                    self.dS.inc(q.dma_start(out=GAMC[:, c:c + 1],
                                            in_=self.inp["gamcol"][c * 128:(c + 1) * 128, :]), 16)
                    self.dS.inc(q.dma_start(out=BETC[:, c:c + 1],
                                            in_=self.inp["betcol"][c * 128:(c + 1) * 128, :]), 16)
            gb_ld = self.dS.n

            y0_mark = []
            xw = lambda eng, m=self.marks["x_loaded"]: eng.wait_ge(self.dG.h, m)
            for s in range(NSL):
                sl = slice(s * SL, (s + 1) * SL)
                m = self.ln_plain(lambda c, sl=sl: self.X[:, c, sl],
                                  lambda c, sl=sl: Y0[:, c, sl],
                                  self.EPS[0:1, 0:1], PSTA, PSTB, P2, P3, extra_wait=xw,
                                  src_is_bf=True)
                y0_mark.append(m)

            qk_evac_v = [0, 0]
            self.scacc_v = 0
            for i in range(NCH):
                csl = slice(i * 128, (i + 1) * 128)
                pq = P0 if i % 2 == 0 else P2
                pk = P1 if i % 2 == 0 else P3
                pi = i % 2

                @blk.tensor
                def _(t, i=i, csl=csl, pq=pq, pk=pk, pi=pi):
                    t.wait_ge(self.dG.h, wqk_ld)
                    t.wait_ge(self.sV.h, wqklo_done)
                    t.wait_ge(self.sV.h, y0_mark[i // 4])
                    t.wait_ge(self.sV.h, self.pb_free_v)
                    if qk_evac_v[pi]:
                        t.wait_ge(self.sV.h, qk_evac_v[pi])
                    for c in range(NDC):
                        t.matmul(pq[:, :], Y0[:, c, csl], WQK[:, c, 0:512],
                                 start=(c == 0), stop=False)
                    for c in range(NDC):
                        t.matmul(pq[:, :], Y0[:, c, csl], WQKLO[:, c, 0:512],
                                 start=False, stop=False)
                    t.matmul(pq[:, :], self.ONER_BF[0:1, 0:128], BQKV[0:1, 0:512],
                             start=False, stop=True)
                    for c in range(NDC):
                        t.matmul(pk[:, :], Y0[:, c, csl], WQK[:, c, 512:1024],
                                 start=(c == 0), stop=False)
                    for c in range(NDC):
                        t.matmul(pk[:, :], Y0[:, c, csl], WQKLO[:, c, 512:1024],
                                 start=False, stop=False)
                    ins = t.matmul(pk[:, :], self.ONER_BF[0:1, 0:128], BQKV[0:1, 512:1024],
                                   start=False, stop=True)
                    self.sT.inc(ins)
                qk_t = self.sT.n

                @blk.vector
                def _(v, qk_t=qk_t, pq=pq, pk=pk, pi=pi):
                    v.wait_ge(self.sT.h, qk_t)
                    v.tensor_copy(QC[:, pi, :], pq[:, :])
                    ins = v.tensor_copy(KC[:, pi, :], pk[:, :])
                    self.sV.inc(ins)
                qk_evac_v[pi] = self.sV.n
                qk_v = self.sV.n

                @blk.tensor
                def _(t, i=i, qk_v=qk_v, pi=pi):
                    t.wait_ge(self.sV.h, qk_v)
                    if self.scacc_v:
                        t.wait_ge(self.sV.h, self.scacc_v)
                    for h in range(NH):
                        ins = t.matmul(PS_SC[0:64, h * 64:(h + 1) * 64],
                                       QC[:, pi, h * 64:(h + 1) * 64],
                                       KC[:, pi, h * 64:(h + 1) * 64],
                                       start=True, stop=True)
                    self.sT.inc(ins)
                sc_t = self.sT.n

                @blk.vector
                def _(v, i=i, sc_t=sc_t):
                    v.wait_ge(self.sT.h, sc_t)
                    if i == 0:
                        ins = v.tensor_copy(SC_ACC[:, :], PS_SC[:, :])
                    else:
                        ins = v.tensor_tensor(SC_ACC[:, :], PS_SC[:, :],
                                              SC_ACC[:, :], ALU.add)
                    self.sV.inc(ins)
                self.scacc_v = self.sV.n
            scores_t = self.sT.n

            @blk.vector
            def _(v):
                v.tensor_scalar_mul(SCF[:, :], SC_ACC[:, :], 0.125)
                ins = v.tensor_copy(SCP[:, :], SCF[:, :])
                self.sV.inc(ins)
            scf_v = self.sV.n

            @blk.gpsimd
            def _(g):
                g.wait_ge(self.sV.h, scf_v)
                self.dG.inc(g.dma_start(out=ccin.ap()[:, :], in_=SCF[:, :]), 16)
                g.wait_ge(self.dG.h, self.dG.n)
                self.sC.inc(g.collective_compute(
                    "AllReduce", ALU.add,
                    replica_groups=[list(range(NCORES))],
                    ins=[ccin.ap().opt()], outs=[ccout.ap().opt()]))
                g.wait_ge(self.sC.h, self.sC.n)
                self.dG.inc(g.dma_start(out=SCF[:, :], in_=ccout.ap()[:, :]), 16)
            scores_rld = self.dG.n

            vv_evac = [qk_evac_v[0], qk_evac_v[1]]
            for vo in range(4):
                for s in range(NSL):
                    p = P0 if (vo * NSL + s) % 2 == 0 else P1
                    ppi = (vo * NSL + s) % 2

                    @blk.tensor
                    def _(t, vo=vo, s=s, p=p, ppi=ppi):
                        t.wait_ge(self.sV.h, y0_mark[s])
                        if vv_evac[ppi]:
                            t.wait_ge(self.sV.h, vv_evac[ppi])
                        sl = slice(s * SL, (s + 1) * SL)
                        for c in range(NDC):
                            t.matmul(p[:, :], WV[:, c, vo * 128:(vo + 1) * 128],
                                     Y0[:, c, sl], start=(c == 0), stop=False)
                        ins = t.matmul(p[:, :], BQKV[0:1, 1024 + vo * 128:1024 + (vo + 1) * 128],
                                       self.ONER_BF[:], start=False, stop=True)
                        self.sT.inc(ins)
                    vmt = self.sT.n

                    @blk.vector
                    def _(v, vo=vo, s=s, p=p, vmt=vmt):
                        v.wait_ge(self.sT.h, vmt)
                        sl = slice(s * SL, (s + 1) * SL)
                        ins = v.tensor_copy(VBUF[:, vo, sl], p[:, :])
                        self.sV.inc(ins)
                    vv_evac[ppi] = self.sV.n
            v_ready = self.sV.n
            v_mm_t = self.sT.n
            st_qkv.close()
            WFC = self.sb("WFC", [128, 4, 1024], BF16, st)
            BFC = self.sb("BFC", [1, 1024], BF16, st)

            @blk.gpsimd
            def _(g):
                g.wait_ge(self.sT.h, v_mm_t)
                for k in range(4):
                    self.dG.inc(g.dma_start(out=WFC[:, k, :],
                                            in_=self.inp["wfc"][k * 128:(k + 1) * 128, :]), 16)
                self.dG.inc(g.dma_start(out=BFC[:], in_=self.inp["bfc"][:, :]), 16)
            wfc_ld = self.dG.n

            @blk.vector
            def _(v):
                v.wait_ge(self.dG.h, scores_rld)
                for h in range(NH):
                    hsl = slice(h * 64, (h + 1) * 64)
                    ins = v.reduce_max(MAXC[:, h:h + 1], SCF[:, hsl], axis=AX.X)
                self.sV.inc(ins)
            mx_raw = self.sV.n

            @blk.vector
            def _(v):
                v.wait_ge(self.sV.h, mx_raw)
                ins = v.tensor_scalar_mul(MAXC[:, :], MAXC[:, :], -1.0)
                self.sV.inc(ins)
            max_v = self.sV.n

            @blk.scalar
            def _(s):
                s.wait_ge(self.sV.h, max_v)
                for h in range(NH):
                    hsl = slice(h * 64, (h + 1) * 64)
                    ins = s.activation(ATTF[:, hsl], SCF[:, hsl], AF.Exp,
                                       bias=MAXC[:, h:h + 1], accum_out=SUMC[:, h:h + 1])
                self.sS.inc(ins)
            exp_s = self.sS.n

            @blk.vector
            def _(v):
                v.wait_ge(self.sS.h, exp_s)
                ins = v.reciprocal(RSUM[:, :], SUMC[:, :])
                self.sV.inc(ins)
            rs_v = self.sV.n

            @blk.vector
            def _(v):
                v.wait_ge(self.sV.h, rs_v)
                for h in range(NH):
                    hsl = slice(h * 64, (h + 1) * 64)
                    ins = v.tensor_scalar_mul(ATT[0:64, hsl], ATTF[:, hsl], RSUM[:, h:h + 1])
                self.sV.inc(ins)
            att_v = self.sV.n

            @blk.gpsimd
            def _(g):
                g.wait_ge(self.sV.h, att_v)
                self.dG.inc(g.dma_start(out=ATT[64:128, :], in_=ATT[0:64, :]), 16)
            att_g = self.dG.n

            o_evac = [0, 0]
            u_evac = [0, 0]
            self.ob_consumed_t = 0
            for s in range(NSL):
                sl = slice(s * SL, (s + 1) * SL)

                for h in range(NH):
                    p = P0 if h % 2 == 0 else P1
                    po = 64 * (h % 2)

                    @blk.tensor
                    def _(t, h=h, p=p, po=po, s=s, sl=sl):
                        t.wait_ge(self.dG.h, att_g)
                        t.wait_ge(self.sV.h, v_ready)
                        if o_evac[h % 2]:
                            t.wait_ge(self.sV.h, o_evac[h % 2])
                        ins = t.matmul(p[po:po + 64, :], ATT[po:po + 64, h * 64:(h + 1) * 64],
                                       VBUF[po:po + 64, h // 2, sl], start=True, stop=True)
                        self.sT.inc(ins)
                    omt = self.sT.n

                    @blk.vector
                    def _(v, h=h, p=p, po=po, omt=omt):
                        v.wait_ge(self.sT.h, max(omt, self.ob_consumed_t))
                        ins = v.tensor_copy(OB[po:po + 64, h // 2, :], p[po:po + 64, :])
                        self.sV.inc(ins)
                    o_evac[h % 2] = self.sV.n
                ob_v = self.sV.n

                for oc in range(NDC):
                    p = P2 if oc % 2 == 0 else P3

                    @blk.tensor
                    def _(t, oc=oc, p=p, ob_v=ob_v):
                        t.wait_ge(self.dG.h, wfc_ld)
                        t.wait_ge(self.sV.h, ob_v)
                        t.wait_ge(self.sV.h, self.pb_free_v)
                        if u_evac[oc % 2]:
                            t.wait_ge(self.sV.h, u_evac[oc % 2])
                        for k in range(4):
                            t.matmul(p[:, :], WFC[:, k, oc * 128:(oc + 1) * 128],
                                     OB[:, k, :], start=(k == 0), stop=False)
                        ins = t.matmul(p[:, :], BFC[0:1, oc * 128:(oc + 1) * 128],
                                       self.ONER_BF[:], start=False, stop=True)
                        self.sT.inc(ins)
                    umt = self.sT.n

                    @blk.vector
                    def _(v, oc=oc, p=p, umt=umt, sl=sl):
                        v.wait_ge(self.sT.h, max(umt, self.scr_free_t))
                        ins = v.tensor_tensor(UB[:, oc, :], p[:, :], Y0[:, oc, sl], ALU.add)
                        self.sV.inc(ins)
                    u_evac[oc % 2] = self.sV.n
                u_v = self.sV.n
                self.ob_consumed_t = self.sT.n

                def gb_w(eng, m=gb_ld, uv=u_v):
                    eng.wait_ge(self.dS.h, m)
                    eng.wait_ge(self.sV.h, uv)
                inner = self.ln_stats_plain(lambda c: UB[:, c, :], self.EPS[0:1, 1:2],
                                            0, 1, PSTA, PSTB, extra_wait=gb_w,
                                            src_is_bf=True)
                bm = self.bcast2(inner, 0, 1, P2, P3, self.pb_free_v)

                @blk.vector
                def _(v, bm=bm):
                    v.wait_ge(self.sT.h, bm)
                    for c in range(NDC):
                        v.tensor_tensor(self.SCR_F1[:, :], UB[:, c, :], P2[:, 0:SL], ALU.mult)
                        v.tensor_tensor(SCR_F2[:, :], self.SCR_F1[:, :], P3[:, 0:SL], ALU.add)
                        ins = v.tensor_scalar(UB[:, c, :], SCR_F2[:, :],
                                              GAMC[:, c:c + 1], BETC[:, c:c + 1],
                                              ALU.mult, ALU.add)
                    self.sV.inc(ins)
                self.pb_free_v = self.sV.n
                affine_v = self.sV.n

                aff_w = lambda eng, m=affine_v: eng.wait_ge(self.sV.h, m)
                self.ln_residual(lambda c: UB[:, c, :], sl, self.EPS[0:1, 0:1],
                                 PSTA, PSTB, P2, P3, src_is_bf=True, extra_wait=aff_w)

            if DEBUG_TAPS and N_BLOCKS == 1:
                self.dbg["d_y0"] = (lambda: Y0[:, :, :].rearrange("p a b -> p (a b)"),
                                    [128, NDC * T], BF16)
                self.dbg["d_scf"] = (lambda: SCF[:, :], [64, 512], F32)
                self.dbg["d_scp"] = (lambda: SCP[:, :], [64, 512], F32)
                self.dbg["d_qc"] = (lambda: QC[:, :, :].rearrange("p a b -> p (a b)"),
                                    [128, 1024], F32)
                self.dbg["d_kc"] = (lambda: KC[:, :, :].rearrange("p a b -> p (a b)"),
                                    [128, 1024], F32)
                self.dbg["d_maxc"] = (lambda: MAXC[:, :], [64, 8], F32)
                self.dbg["d_sumc"] = (lambda: SUMC[:, :], [64, 8], F32)
                self.dbg["d_attf"] = (lambda: ATTF[:, :], [64, 512], F32)
                self.dbg["d_att"] = (lambda: ATT[:, :], [128, 512], BF16)
                self.dbg["d_v"] = (lambda: VBUF[:, :, :].rearrange("p a b -> p (a b)"),
                                   [128, 4 * T], BF16)
                self.dbg["d_ob"] = (lambda: OB[:, :, :].rearrange("p a b -> p (a b)"),
                                    [128, 4 * SL], BF16)
                self.dbg["d_ub"] = (lambda: UB[:, :, :].rearrange("p a b -> p (a b)"),
                                    [128, NDC * SL], BF16)
                self.emit_taps()
    def phase_ffn(self):
        blk = self.block
        st = ExitStack()
        with st:
            A = self.sb("A", [128, 32, 1024], BF16, st)
            H1 = self.sb("H1", [128, 32, 1024], BF16, st)
            SCR_FF = self.sb("SCR_FF", [128, 512], F32, st)
            SCLS = self.sb("SCLS", [128, 4, 32], F32, st)
            SCL3 = self.sb("SCL3", [128, 8], F32, st)

            PA1 = self.ps("PA1", [128, 512], F32, st)
            PA2 = self.ps("PA2", [128, 512], F32, st)
            PH1 = self.ps("PH1", [128, 512], F32, st)
            PH2 = self.ps("PH2", [128, 512], F32, st)
            PB1 = self.ps("PB1f", [128, 512], F32, st)
            PB2 = self.ps("PB2f", [128, 512], F32, st)
            PSTA = self.ps("PSTAf", [1, 512], F32, st)
            PSTB = self.ps("PSTBf", [1, 512], F32, st)

            barrier = dict(T=self.sT.n, V=self.sV.n, S=self.sS.n)

            if DEBUG_TAPS:
                xsnap = self.dout("xsnap", [D, T], F32)

                @blk.gpsimd
                def _(g):
                    g.wait_ge(self.sT.h, barrier["T"])
                    g.wait_ge(self.sV.h, barrier["V"])
                    g.wait_ge(self.sS.h, barrier["S"])
                    for c in range(NDC):
                        self.dG.inc(g.dma_start(out=xsnap[c * 128:(c + 1) * 128, :],
                                                in_=self.X[:, c, :]), 16)

            @blk.sync
            def _(q):
                q.wait_ge(self.sV.h, barrier["V"])
                for i, nm in enumerate(("s1", "s2", "hs1", "hs2")):
                    self.dS.inc(q.dma_start(out=SCLS[:, i, :], in_=self.inp[nm][:, :]), 16)
                self.dS.inc(q.dma_start(out=SCL3[:, :], in_=self.inp["s3"][:, :]), 16)
            scls_ld = self.dS.n

            self.pa1_free_s = barrier["S"]
            self.pa2_free_v = barrier["V"]
            self.scrff_free_v = barrier["V"]
            self.u_free_v = barrier["V"]
            ph_free = [barrier["S"], barrier["S"]]

            for hf in range(2):
                hoff = hf * 1024
                gpsimd_fence = False
                st12 = ExitStack()
                YH = self.sb(f"YH{hf}", [128, NDC, 1024], BF16, st12)
                W12B = self.sb(f"W12B{hf}", [128, 2, 2, NDC, 128], BF16, st12)
                w12_free = [self.sT.n, self.sT.n]
                yh_mark = []
                bw = lambda eng, m=self.sV.n: eng.wait_ge(self.sV.h, m)
                for s2 in range(2):
                    sl = slice(hoff + s2 * 512, hoff + (s2 + 1) * 512)
                    dsl = slice(s2 * 512, (s2 + 1) * 512)
                    m = self.ln_plain(lambda c, sl=sl: self.X[:, c, sl],
                                      lambda c, dsl=dsl: YH[:, c, dsl],
                                      self.EPS[0:1, 0:1], PSTA, PSTB, PB1, PB2,
                                      extra_wait=bw, src_is_bf=True)
                    yh_mark.append(m)

                for od in range(32):
                    p = od % 2

                    @blk.gpsimd
                    def _(g, od=od, p=p, fence=(not gpsimd_fence),
                          fv=self.sV.n, fs=self.sS.n):
                        if fence:
                            g.wait_ge(self.sV.h, fv)
                            g.wait_ge(self.sS.h, fs)
                        g.wait_ge(self.sT.h, w12_free[p])
                        for c in range(NDC):
                            self.dG.inc(g.dma_start(
                                out=W12B[:, p, 0, c, :],
                                in_=self.inp["w1T"][c * 128:(c + 1) * 128,
                                                    od * 128:(od + 1) * 128]), 16)
                            self.dG.inc(g.dma_start(
                                out=W12B[:, p, 1, c, :],
                                in_=self.inp["w2T"][c * 128:(c + 1) * 128,
                                                    od * 128:(od + 1) * 128]), 16)
                    gpsimd_fence = True
                    w12_ld = self.dG.n

                    for sl2 in range(2):
                        tsl = slice(sl2 * 512, (sl2 + 1) * 512)

                        @blk.tensor
                        def _(t, od=od, p=p, sl2=sl2, tsl=tsl, w12_ld=w12_ld):
                            t.wait_ge(self.dG.h, w12_ld)
                            t.wait_ge(self.sV.h, yh_mark[sl2])
                            t.wait_ge(self.sS.h, self.pa1_free_s)
                            t.wait_ge(self.sV.h, self.pa2_free_v)
                            for c in range(NDC):
                                t.matmul(PA1[:, :], W12B[:, p, 0, c, :], YH[:, c, tsl],
                                         start=(c == 0), stop=(c == NDC - 1))
                            for c in range(NDC):
                                ins = t.matmul(PA2[:, :], W12B[:, p, 1, c, :], YH[:, c, tsl],
                                               start=(c == 0), stop=(c == NDC - 1))
                            self.sT.inc(ins)
                        pa_t = self.sT.n

                        @blk.scalar
                        def _(s, od=od, pa_t=pa_t):
                            s.wait_ge(self.sT.h, pa_t)
                            s.wait_ge(self.dS.h, scls_ld)
                            s.wait_ge(self.sV.h, self.scrff_free_v)
                            ins = s.activation(SCR_FF[:, :], PA1[:, :], AF.Silu,
                                               scale=SCLS[:, 0, od:od + 1])
                            self.sS.inc(ins)
                        self.pa1_free_s = self.sS.n
                        t1_s = self.sS.n

                        @blk.vector
                        def _(v, od=od, tsl=tsl, t1_s=t1_s):
                            v.wait_ge(self.sS.h, t1_s)
                            ins = v.scalar_tensor_tensor(A[:, od, tsl], PA2[:, :],
                                                         SCLS[:, 1, od:od + 1],
                                                         SCR_FF[:, :],
                                                         ALU.mult, ALU.mult)
                            self.sV.inc(ins)
                        self.pa2_free_v = self.sV.n
                        self.scrff_free_v = self.sV.n
                    w12_free[p] = self.sT.n
                a_done = self.sV.n

                st12.close()
                st345 = ExitStack()
                WB = self.sb(f"WB{hf}", [128, 2, 32, 128], BF16, st345)
                U = self.sb(f"U{hf}", [128, NDC, 512], BF16, st345)
                wb_free = [self.sT.n, self.sT.n]

                for step, (wname, scol, src, dst) in enumerate((
                        ("h1T", 2, A, H1), ("h2T", 3, H1, A))):
                    src_mark = a_done if step == 0 else self.sS.n
                    for od in range(32):
                        p = od % 2

                        @blk.gpsimd
                        def _(g, wname=wname, od=od, p=p):
                            g.wait_ge(self.sT.h, wb_free[p])
                            for c in range(32):
                                self.dG.inc(g.dma_start(
                                    out=WB[:, p, c, :],
                                    in_=self.inp[wname][c * 128:(c + 1) * 128,
                                                        od * 128:(od + 1) * 128]), 16)
                        wb_ld = self.dG.n

                        for sl2 in range(2):
                            tsl = slice(sl2 * 512, (sl2 + 1) * 512)
                            k = (od * 2 + sl2) % 2
                            ph = PH1 if k == 0 else PH2

                            @blk.tensor
                            def _(t, od=od, p=p, tsl=tsl, k=k, ph=ph,
                                  wb_ld=wb_ld, src=src, src_mark=src_mark):
                                t.wait_ge(self.dG.h, wb_ld)
                                if step == 0:
                                    t.wait_ge(self.sV.h, src_mark)
                                else:
                                    t.wait_ge(self.sS.h, src_mark)
                                t.wait_ge(self.sS.h, ph_free[k])
                                for c in range(32):
                                    ins = t.matmul(ph[:, :], WB[:, p, c, :],
                                                   src[:, c, tsl],
                                                   start=(c == 0), stop=(c == 31))
                                self.sT.inc(ins)
                            h_t = self.sT.n

                            @blk.scalar
                            def _(s, od=od, tsl=tsl, ph=ph, scol=scol, h_t=h_t,
                                  dst=dst):
                                s.wait_ge(self.sT.h, h_t)
                                ins = s.activation(dst[:, od, tsl], ph[:, :], AF.Gelu,
                                                   scale=SCLS[:, scol, od:od + 1])
                                self.sS.inc(ins)
                            ph_free[k] = self.sS.n
                        wb_free[p] = self.sT.n
                h2_done = self.sS.n

                for sl2 in range(2):
                    tsl = slice(sl2 * 512, (sl2 + 1) * 512)
                    for od in range(8):
                        p = od % 2

                        @blk.gpsimd
                        def _(g, od=od, p=p):
                            g.wait_ge(self.sT.h, wb_free[p])
                            for c in range(32):
                                self.dG.inc(g.dma_start(
                                    out=WB[:, p, c, :],
                                    in_=self.inp["w3T"][c * 128:(c + 1) * 128,
                                                        od * 128:(od + 1) * 128]), 16)
                        wb_ld = self.dG.n

                        k = od % 2
                        ph = PH1 if k == 0 else PH2

                        @blk.tensor
                        def _(t, od=od, p=p, tsl=tsl, k=k, ph=ph,
                              wb_ld=wb_ld, h2_done=h2_done):
                            t.wait_ge(self.dG.h, wb_ld)
                            t.wait_ge(self.sS.h, h2_done)
                            t.wait_ge(self.sS.h, ph_free[k])
                            for c in range(32):
                                ins = t.matmul(ph[:, :], WB[:, p, c, :], A[:, c, tsl],
                                               start=(c == 0), stop=(c == 31))
                            self.sT.inc(ins)
                        o_t = self.sT.n

                        @blk.scalar
                        def _(s, od=od, ph=ph, o_t=o_t, U=U):
                            s.wait_ge(self.sT.h, o_t)
                            s.wait_ge(self.sV.h, self.u_free_v)
                            ins = s.activation(U[:, od, :], ph[:, :], AF.Copy,
                                               scale=SCL3[:, od:od + 1])
                            self.sS.inc(ins)
                        ph_free[k] = self.sS.n
                        wb_free[p] = self.sT.n
                    u_mark = self.sS.n

                    uw = lambda eng, m=u_mark: eng.wait_ge(self.sS.h, m)
                    self.ln_residual(lambda c, U=U: U[:, c, :],
                                     slice(hoff + sl2 * 512, hoff + (sl2 + 1) * 512),
                                     self.EPS[0:1, 0:1], PSTA, PSTB, PB1, PB2,
                                     extra_wait=uw, src_is_bf=True)
                    self.u_free_v = self.sV.n
                if DEBUG_TAPS and hf == 1:
                    self.dbg["f_h2"] = (lambda A=A: A[:, :, :].rearrange("p a b -> p (a b)"),
                                        [128, 32 * 1024], BF16)
                    self.dbg["f_h1"] = (lambda H1=H1: H1[:, :, :].rearrange("p a b -> p (a b)"),
                                        [128, 32 * 1024], BF16)
                    self.dbg["f_u"] = (lambda U=U: U[:, :, :].rearrange("p a b -> p (a b)"),
                                       [128, NDC * 512], BF16)
                    self.emit_taps()
                st345.close()

    def phase_mamba(self):
        blk = self.block
        nc = self.nc
        NCH16 = 16
        st = ExitStack()
        with st:
            XC = self.sb("XC", [128, NCH16, T], BF16, st)
            UTB = self.sb("UTB", [128, 128], F32, st)
            GDIFB = self.sb("GDIFB", [128, 1024], F32, st)
            TILE8B = self.sb("TILE8B", [128, 1024], F32, st)
            LVLB = self.sb("LVLB", [128, 1024], F32, st)
            SEL127B = self.sb("SEL127B", [128, 1], F32, st)
            HENDDIFB = self.sb("HENDDIFB", [128, 128], F32, st)
            IDF = self.sb("IDF", [128, 128], F32, st)
            ALOGB = self.sb("ALOGB", [64, 1], F32, st)
            CONVWB = self.sb("CONVWB", [128, 64], F32, st)
            CONVBB = self.sb("CONVBB", [128, 16], F32, st)
            DVECB = self.sb("DVECB", [128, 16], F32, st)
            CMASK = self.sb("CMASK", [64, 8], F32, st)
            HMASK = self.sb("HMASK", [128, 8], F32, st)
            ONE1 = self.sb("ONE1", [128, 1], F32, st)
            HLHS = self.sb("HLHS", [2, 64], F32, st)
            DTRAW = self.sb("DTRAW", [128, 16], F32, st)
            DTC = self.sb("DTC", [128, 16], F32, st)
            LNDTC = self.sb("LNDTC", [128, 16], F32, st)
            SRELC = self.sb("SRELC", [128, 16], F32, st)
            SENDR = self.sb("SENDR", [1, 16], F32, st)
            STOT = self.sb("STOT", [1, 1], F32, st)
            DKC = self.sb("DKC", [64, 1], F32, st)
            DCOL = self.sb("DCOL", [64, 1], F32, st)

            marks = dict(T=self.sT.n, V=self.sV.n, S=self.sS.n)

            halo_in = nc.dram_tensor("halo_in", [128, 48], F32, kind="Internal")
            halo_out = nc.dram_tensor("halo_out", [1024, 48], F32, kind="Internal")
            ag_in = nc.dram_tensor("ag_in", [64, 2049], F32, kind="Internal")
            ag_out = nc.dram_tensor("ag_out", [512, 2049], F32, kind="Internal")

            @blk.sync
            def _(q):
                q.wait_ge(self.sV.h, marks["V"])
                q.wait_ge(self.sS.h, marks["S"])
                for nm, dst in (("ut", UTB), ("gdif", GDIFB), ("tile8", TILE8B),
                                ("lvlmask", LVLB), ("sel127", SEL127B),
                                ("henddif", HENDDIFB), ("identf", IDF),
                                ("alog", ALOGB), ("convw", CONVWB),
                                ("convb", CONVBB), ("dvec", DVECB),
                                ("cmask", CMASK), ("hmask", HMASK)):
                    self.dS.inc(q.dma_start(out=dst[:], in_=self.inp[nm][:, :]), 16)
            mconst_ld = self.dS.n

            stM12 = ExitStack()
            YGS = self.sb("YGS", [128, NDC, 512], BF16, stM12)
            WINB = self.sb("WINB", [128, 2, NDC, 128], BF16, stM12)
            XIN = self.sb("XIN", [128, NCH16, 2052], BF16, stM12)
            PM1 = self.ps("PM1", [128, 512], F32, stM12)
            PM2 = self.ps("PM2", [128, 512], F32, stM12)
            PB1m = self.ps("PB1m", [128, 512], F32, stM12)
            PB2m = self.ps("PB2m", [128, 512], F32, stM12)
            PSTAm = self.ps("PSTAm", [1, 512], F32, stM12)
            PSTBm = self.ps("PSTBm", [1, 512], F32, stM12)
            PTRA = self.ps("PTRA", [1, 64], F32, stM12)

            @blk.vector
            def _(v):
                v.wait_ge(self.sS.h, marks["S"])
                ins = v.memset(ONE1[:], 1.0)
                self.sV.inc(ins)
            one_v = self.sV.n

            @blk.scalar
            def _(s):
                s.wait_ge(self.dS.h, mconst_ld)
                ins = s.activation(DKC[:, :], ALOGB[:, :], AF.Exp)
                self.sS.inc(ins)
            expa_s = self.sS.n

            @blk.vector
            def _(v):
                v.wait_ge(self.sS.h, expa_s)
                ins = v.tensor_scalar_mul(DKC[:, :], DKC[:, :], -1.0)
                self.sV.inc(ins)
            acol_v = self.sV.n

            @blk.tensor
            def _(t):
                t.wait_ge(self.sV.h, acol_v)
                t.wait_ge(self.sT.h, marks["T"])
                ins = t.transpose(PTRA[:, :], DKC[:, :], IDF[0:64, 0:64])
                self.sT.inc(ins)
            atr_t = self.sT.n

            @blk.vector
            def _(v):
                v.wait_ge(self.sT.h, atr_t)
                v.wait_ge(self.sV.h, one_v)
                ins = v.tensor_copy(HLHS[0:1, :], PTRA[:, :])
                self.sV.inc(ins)
            arow_v = self.sV.n

            win_free = [marks["T"], marks["T"]]
            pm_free = [marks["V"], marks["V"]]
            xin_marks = []
            for tsl in range(4):
                xsl = slice(tsl * 512, (tsl + 1) * 512)
                bw = lambda eng, m=self.sV.n: eng.wait_ge(self.sV.h, m)
                yg_m = self.ln_plain(lambda c, xsl=xsl: self.X[:, c, xsl],
                                     lambda c: YGS[:, c, :],
                                     self.EPS[0:1, 0:1], PSTAm, PSTBm, PB1m, PB2m,
                                     extra_wait=bw, src_is_bf=True)
                for e in range(16):
                    p = e % 2

                    @blk.gpsimd
                    def _(g, e=e, p=p, first=(tsl == 0 and e == 0)):
                        if first:
                            g.wait_ge(self.sV.h, marks["V"])
                            g.wait_ge(self.sS.h, marks["S"])
                        g.wait_ge(self.sT.h, win_free[p])
                        for c in range(NDC):
                            self.dG.inc(g.dma_start(
                                out=WINB[:, p, c, :],
                                in_=self.inp["winT"][c * 128:(c + 1) * 128,
                                                     e * 128:(e + 1) * 128]), 16)
                    win_ld = self.dG.n

                    pm = PM1 if p == 0 else PM2

                    @blk.tensor
                    def _(t, e=e, p=p, pm=pm, win_ld=win_ld, yg_m=yg_m):
                        t.wait_ge(self.dG.h, win_ld)
                        t.wait_ge(self.sV.h, yg_m)
                        t.wait_ge(self.sV.h, pm_free[p])
                        for c in range(NDC):
                            ins = t.matmul(pm[:, :], WINB[:, p, c, :], YGS[:, c, :],
                                           start=(c == 0), stop=(c == NDC - 1))
                        self.sT.inc(ins)
                    win_free[p] = self.sT.n
                    xz_t = self.sT.n

                    @blk.vector
                    def _(v, e=e, tsl=tsl, pm=pm, xz_t=xz_t):
                        v.wait_ge(self.sT.h, xz_t)
                        ins = v.tensor_copy(
                            XIN[:, e, 4 + tsl * 512:4 + (tsl + 1) * 512], pm[:, :])
                        self.sV.inc(ins)
                    pm_free[p] = self.sV.n
                xin_marks.append(self.sV.n)
            xin_done = self.sV.n

            @blk.gpsimd
            def _(g):
                g.wait_ge(self.sV.h, xin_done)
                for ch in range(NCH16):
                    self.dG.inc(g.dma_start(out=halo_in.ap()[:, ch * 3:(ch + 1) * 3],
                                            in_=XIN[:, ch, 2049:2052]), 16)
                g.wait_ge(self.dG.h, self.dG.n)
                self.sC.inc(g.collective_compute(
                    "AllGather", ALU.bypass,
                    replica_groups=[list(range(NCORES))],
                    ins=[halo_in.ap().opt()], outs=[halo_out.ap().opt()]))
                g.wait_ge(self.sC.h, self.sC.n)
            HALOF = self.sb("HALOF", [128, 48], F32, stM12)
            HSLOT = self.sb("HSLOT", [128, 48], F32, stM12)

            halo_v = 0
            for b in range(NCORES):
                @blk.gpsimd
                def _(g, b=b):
                    if halo_v:
                        g.wait_ge(self.sV.h, halo_v)
                    self.dG.inc(g.dma_start(
                        out=HSLOT[:, :], in_=halo_out.ap()[b * 128:(b + 1) * 128, :]), 16)
                hs_ld = self.dG.n

                @blk.vector
                def _(v, b=b, hs_ld=hs_ld):
                    v.wait_ge(self.dG.h, hs_ld)
                    if b == 0:
                        v.memset(HALOF[:, :], 0.0)
                    ins = v.scalar_tensor_tensor(HALOF[:, :], HSLOT[:, :],
                                                 HMASK[:, b:b + 1], HALOF[:, :],
                                                 ALU.mult, ALU.add)
                    self.sV.inc(ins)
                halo_v = self.sV.n

            @blk.vector
            def _(v):
                v.wait_ge(self.sV.h, halo_v)
                for ch in range(NCH16):
                    ins = v.tensor_copy(XIN[:, ch, 1:4], HALOF[:, ch * 3:(ch + 1) * 3])
                self.sV.inc(ins)
            halo_w = self.sV.n

            conv_s = self.sS.n
            for ch in range(NCH16):
                for tsl in range(4):
                    base = 4 + tsl * 512

                    @blk.vector
                    def _(v, ch=ch, tsl=tsl, base=base, first=(ch == 0 and tsl == 0)):
                        if first:
                            v.wait_ge(self.sV.h, halo_w)
                            v.wait_ge(self.dS.h, mconst_ld)
                        v.wait_ge(self.sS.h, self.sS.n)
                        v.tensor_scalar_mul(self.SCR_F1[:, :],
                                            XIN[:, ch, base - 3:base + 509],
                                            CONVWB[:, ch * 4:ch * 4 + 1])
                        for j in (1, 2, 3):
                            ins = v.scalar_tensor_tensor(
                                self.SCR_F1[:, :],
                                XIN[:, ch, base - 3 + j:base + 509 + j],
                                CONVWB[:, ch * 4 + j:ch * 4 + j + 1],
                                self.SCR_F1[:, :], ALU.mult, ALU.add)
                        self.sV.inc(ins)
                    cv_v = self.sV.n

                    @blk.scalar
                    def _(s, ch=ch, tsl=tsl, cv_v=cv_v):
                        s.wait_ge(self.sV.h, cv_v)
                        ins = s.activation(XC[:, ch, tsl * 512:(tsl + 1) * 512],
                                           self.SCR_F1[:, :], AF.Silu,
                                           bias=CONVBB[:, ch:ch + 1])
                        self.sS.inc(ins)
                    conv_s = self.sS.n
            xc_done = conv_s
            conv_v = self.sV.n
            stM12.close()
            if MSTAGE <= 1:
                return

            stM3 = ExitStack()
            BB = self.sb("BB", [64, T], F32, stM3)
            CC = self.sb("CC", [64, T], F32, stM3)
            HST = self.sb("HST", [64, 2048], F32, stM3)
            HB = self.sb("HB", [64, 2048], BF16, stM3)
            WXB = self.sb("WXB", [128, NCH16, 132], BF16, stM3)
            GBUF = self.sb("GBUF", [64, 8, 128], BF16, stM3)
            HBUF = self.sb("HBUF", [64, 8, 128], BF16, stM3)
            G0B = self.sb("G0B", [64, 128], BF16, stM3)
            HENDTS = self.sb("HENDTS", [128, 64], BF16, stM3)
            XCT = self.sb("XCT", [128, NCH16, 128], BF16, stM3)
            MS = self.sb("MS", [128, 128], F32, stM3)
            TMB = self.sb("TMB", [128, 128], F32, stM3)
            MB = self.sb("MB", [128, 128], BF16, stM3)
            GRB = self.sb("GRB", [1, 1024], F32, stM3)
            HR2 = self.sb("HR2", [1, 1024], F32, stM3)
            TROW = self.sb("TROW", [1, 1024], F32, stM3)
            SRB = self.sb("SRB", [1, 128], F32, stM3)
            AGD = self.sb("AGD", [64, 8], F32, stM3)
            O8 = self.sb("O8", [64, 8], F32, stM3)
            DMB = self.sb("DMB", [64, 8], F32, stM3)
            PFB = self.sb("PFB", [64, 8], F32, stM3)
            WCOL = self.sb("WCOL", [64, 8], F32, stM3)
            AGC = self.sb("AGC", [64, 2, 2048], BF16, stM3)
            TCL = self.sb("TCL", [64, 1024], F32, stM3)

            PROW = self.ps("PROW", [64, 512], F32, stM3)
            PARG = self.ps("PARG", [64, 512], F32, stM3)
            PP1 = self.ps("PP1", [128, 128], F32, stM3)
            PP2 = self.ps("PP2", [128, 128], F32, stM3)
            PXCTa = self.ps("PXCTa", [128, 128], BF16, stM3)
            PXCTb = self.ps("PXCTb", [128, 128], BF16, stM3)
            PY = self.ps("PY", [128, 128], F32, stM3)
            PCC2 = self.ps("PCC2", [64, 512], F32, stM3)

            @blk.gpsimd
            def _(g, tn=self.sT.n, cv=conv_v, cs=xc_done):
                g.wait_ge(self.sT.h, tn)
                g.wait_ge(self.sV.h, cv)
                g.wait_ge(self.sS.h, cs)
                for ch in range(NCH16):
                    self.dG.inc(g.dma_start(
                        out=WXB[:, ch, 0:129],
                        in_=self.inp["wxT"][ch * 128:(ch + 1) * 128, :]), 16)
            wx_ld = self.dG.n

            bc_done = 0
            for tsl in range(4):
                xsl = slice(tsl * 512, (tsl + 1) * 512)

                @blk.tensor
                def _(t, tsl=tsl, xsl=xsl):
                    t.wait_ge(self.dG.h, wx_ld)
                    t.wait_ge(self.sS.h, xc_done)
                    if bc_done:
                        t.wait_ge(self.sV.h, bc_done)
                    for ch in range(NCH16):
                        t.matmul(PARG[:, :], WXB[:, ch, 0:64], XC[:, ch, xsl],
                                 start=(ch == 0), stop=(ch == NCH16 - 1))
                    for ch in range(NCH16):
                        ins = t.matmul(PCC2[:, :], WXB[:, ch, 64:128], XC[:, ch, xsl],
                                       start=(ch == 0), stop=(ch == NCH16 - 1))
                    self.sT.inc(ins)
                bcr_t = self.sT.n

                @blk.vector
                def _(v, tsl=tsl, xsl=xsl, bcr_t=bcr_t):
                    v.wait_ge(self.sT.h, bcr_t)
                    v.tensor_copy(BB[:, xsl], PARG[:, :])
                    ins = v.tensor_copy(CC[:, xsl], PCC2[:, :])
                    self.sV.inc(ins)
                bc_done = self.sV.n

            dtr_marks = 0
            for ch2 in range(2):
                @blk.tensor
                def _(t, ch2=ch2):
                    t.wait_ge(self.sS.h, xc_done)
                    t.wait_ge(self.dG.h, wx_ld)
                    if dtr_marks:
                        t.wait_ge(self.sV.h, dtr_marks)
                    for k in range(8):
                        ch = ch2 * 8 + k
                        for dch in range(NCH16):
                            ins = t.matmul(PP1[:, k:k + 1],
                                           XC[:, dch, ch * 128:(ch + 1) * 128],
                                           WXB[:, dch, 128:129],
                                           start=(dch == 0), stop=(dch == NCH16 - 1))
                    self.sT.inc(ins)
                dtr_t = self.sT.n

                @blk.vector
                def _(v, ch2=ch2, dtr_t=dtr_t):
                    v.wait_ge(self.sT.h, dtr_t)
                    ins = v.tensor_copy(DTRAW[:, ch2 * 8:(ch2 + 1) * 8], PP1[:, 0:8])
                    self.sV.inc(ins)
                dtr_marks = self.sV.n

            @blk.scalar
            def _(s):
                s.wait_ge(self.sV.h, dtr_marks)
                ins = s.activation(DTC[:, :], DTRAW[:, :], AF.Exp)
                self.sS.inc(ins)
            e_s = self.sS.n

            @blk.scalar
            def _(s):
                s.wait_ge(self.sS.h, e_s)
                s.activation(DTC[:, :], DTC[:, :], AF.Ln, bias=ONE1[:, :])
                ins = s.activation(LNDTC[:, :], DTC[:, :], AF.Ln)
                self.sS.inc(ins)
            dt_s = self.sS.n

            @blk.tensor
            def _(t):
                t.wait_ge(self.sS.h, dt_s)
                ins = t.matmul(PP1[:, 0:16], UTB[:, :], DTC[:, :], start=True, stop=True)
                self.sT.inc(ins)
            srl_t = self.sT.n

            @blk.vector
            def _(v):
                v.wait_ge(self.sT.h, srl_t)
                ins = v.tensor_copy(SRELC[:, :], PP1[:, 0:16])
                self.sV.inc(ins)
            srlc_v = self.sV.n

            @blk.tensor
            def _(t):
                t.wait_ge(self.sV.h, srlc_v)
                ins = t.matmul(PROW[0:1, 0:16], SEL127B[:, :], SRELC[:, :],
                               start=True, stop=True)
                self.sT.inc(ins)
            snd_t = self.sT.n

            @blk.vector
            def _(v):
                v.wait_ge(self.sT.h, snd_t)
                ins = v.tensor_copy(SENDR[:, :], PROW[0:1, 0:16])
                self.sV.inc(ins)
            srel_v = self.sV.n

            if MSTAGE <= 2:
                stM3.close()
                return
            mm = {}

            @blk.vector
            def _(v):
                v.wait_ge(self.sV.h, srel_v)
                ins = v.memset(HST[:, :], 0.0)
                self.sV.inc(ins)
            h0_v = self.sV.n

            def emit_pass(pass2):
                prow_free_v = self.sV.n
                prow_free_s = self.sS.n
                parg_free = (self.sS.n, self.sV.n)
                hupd_v = self.sV.n
                xct_v = [self.sV.n, self.sV.n]
                ptr_free_v = self.sV.n
                y_prev_v = self.sV.n
                for ch in range(NCH16):
                    chsl = slice(ch * 128, (ch + 1) * 128)
                    scol = SRELC[:, ch:ch + 1]


                    if pass2:
                        @blk.vector
                        def _(v, hupd_v=hupd_v):
                            v.wait_ge(self.sV.h, hupd_v)
                            ins = v.tensor_copy(HB[:, :], HST[:, :])
                            self.sV.inc(ins)
                        mm["hb"] = self.sV.n

                        @blk.tensor
                        def _(t, scol=scol, pf=prow_free_v, pfs=prow_free_s):
                            t.wait_ge(self.sV.h, pf)
                            t.wait_ge(self.sS.h, pfs)
                            ins = t.matmul(PROW[0:1, 0:512], scol, GDIFB[:, 0:512],
                                           start=True, stop=True)
                            self.sT.inc(ins)
                        m1 = self.sT.n

                        @blk.vector
                        def _(v, m1=m1):
                            v.wait_ge(self.sT.h, m1)
                            ins = v.tensor_copy(GRB[0:1, 0:512], PROW[0:1, 0:512])
                            self.sV.inc(ins)
                        m1v = self.sV.n

                        @blk.tensor
                        def _(t, scol=scol, m1v=m1v):
                            t.wait_ge(self.sV.h, m1v)
                            ins = t.matmul(PROW[0:1, 0:512], scol, GDIFB[:, 512:1024],
                                           start=True, stop=True)
                            self.sT.inc(ins)
                        m2 = self.sT.n

                        @blk.vector
                        def _(v, m2=m2):
                            v.wait_ge(self.sT.h, m2)
                            v.tensor_copy(GRB[0:1, 512:1024], PROW[0:1, 0:512])
                            ins = v.tensor_scalar_mul(HR2[0:1, :], GRB[0:1, :], -1.0)
                            self.sV.inc(ins)
                        m2v = self.sV.n

                        @blk.tensor
                        def _(t, ch=ch, m2v=m2v):
                            t.wait_ge(self.sV.h, m2v)
                            ins = t.matmul(PROW[0:1, 0:512], LNDTC[:, ch:ch + 1],
                                           TILE8B[:, 0:512], start=True, stop=True)
                            self.sT.inc(ins)
                        m3 = self.sT.n

                        @blk.vector
                        def _(v, m3=m3):
                            v.wait_ge(self.sT.h, m3)
                            ins = v.tensor_copy(TROW[0:1, 0:512], PROW[0:1, 0:512])
                            self.sV.inc(ins)
                        m3v = self.sV.n

                        @blk.tensor
                        def _(t, ch=ch, m3v=m3v):
                            t.wait_ge(self.sV.h, m3v)
                            ins = t.matmul(PROW[0:1, 0:512], LNDTC[:, ch:ch + 1],
                                           TILE8B[:, 512:1024], start=True, stop=True)
                            self.sT.inc(ins)
                        m4 = self.sT.n

                        @blk.vector
                        def _(v, m4=m4):
                            v.wait_ge(self.sT.h, m4)
                            ins = v.tensor_copy(TROW[0:1, 512:1024], PROW[0:1, 0:512])
                            self.sV.inc(ins)
                        m4v = self.sV.n
                        prow_free_v = m4v

                        gh_done = []
                        for half in range(2):
                            csl = slice(half * 512, (half + 1) * 512)

                            @blk.tensor
                            def _(t, csl=csl, m2v=m2v,
                                  pfs=self.sS.n, pfv=self.sV.n):
                                t.wait_ge(self.sV.h, m2v)
                                t.wait_ge(self.sS.h, pfs)
                                t.wait_ge(self.sV.h, pfv)
                                ins = t.matmul(PARG[:, :], HLHS[0:1, :], GRB[0:1, csl],
                                               start=True, stop=True)
                                self.sT.inc(ins)
                            g_t = self.sT.n

                            @blk.vector
                            def _(v, g_t=g_t):
                                v.wait_ge(self.sT.h, g_t)
                                ins = v.tensor_scalar(TCL[:, 0:512], PARG[:, :],
                                                      15.0, None, ALU.min)
                                self.sV.inc(ins)
                            gcl_v = self.sV.n

                            @blk.scalar
                            def _(s, half=half, gcl_v=gcl_v):
                                s.wait_ge(self.sV.h, gcl_v)
                                for l in range(4):
                                    ins = s.activation(GBUF[:, half * 4 + l, :],
                                                       TCL[:, l * 128:(l + 1) * 128],
                                                       AF.Exp)
                                self.sS.inc(ins)
                            g_s = self.sS.n

                            @blk.tensor
                            def _(t, csl=csl, gcl_v=gcl_v, m4v=m4v):
                                t.wait_ge(self.sV.h, gcl_v)
                                t.wait_ge(self.sV.h, m4v)
                                t.matmul(PARG[:, :], HLHS[0:1, :], HR2[0:1, csl],
                                         start=True, stop=False)
                                ins = t.matmul(PARG[:, :], self.ONER128_F[0:1, 0:64],
                                               TROW[0:1, csl], start=False, stop=True)
                                self.sT.inc(ins)
                            h_t = self.sT.n

                            @blk.vector
                            def _(v, h_t=h_t, g_s=g_s):
                                v.wait_ge(self.sT.h, h_t)
                                v.wait_ge(self.sS.h, g_s)
                                ins = v.tensor_scalar(TCL[:, 512:1024], PARG[:, :],
                                                      15.0, None, ALU.min)
                                self.sV.inc(ins)
                            hcl_v = self.sV.n

                            @blk.scalar
                            def _(s, half=half, hcl_v=hcl_v):
                                s.wait_ge(self.sV.h, hcl_v)
                                for l in range(4):
                                    ins = s.activation(HBUF[:, half * 4 + l, :],
                                                       TCL[:, 512 + l * 128:512 + (l + 1) * 128],
                                                       AF.Exp)
                                self.sS.inc(ins)
                            gh_done.append(self.sS.n)
                        parg_free = (self.sS.n, self.sV.n)

                        @blk.vector
                        def _(v, ch=ch, chsl=chsl, m=gh_done[-1]):
                            v.wait_ge(self.sS.h, m)
                            cbc = CC[:, chsl].unsqueeze(1).to_broadcast([64, 8, 128])
                            bbc = BB[:, chsl].unsqueeze(1).to_broadcast([64, 8, 128])
                            v.tensor_tensor(GBUF[:, :, :], GBUF[:, :, :], cbc, ALU.mult)
                            ins = v.tensor_tensor(HBUF[:, :, :], HBUF[:, :, :], bbc,
                                                  ALU.mult)
                            self.sV.inc(ins)
                        ghm_v = self.sV.n

                        @blk.tensor
                        def _(t, scol=scol, pf=prow_free_v):
                            t.wait_ge(self.sV.h, pf)
                            ins = t.matmul(PROW[0:1, 0:128], scol, IDF[:, :],
                                           start=True, stop=True)
                            self.sT.inc(ins)
                        sr_t = self.sT.n

                        @blk.vector
                        def _(v, sr_t=sr_t):
                            v.wait_ge(self.sT.h, sr_t)
                            ins = v.tensor_copy(SRB[:, :], PROW[0:1, 0:128])
                            self.sV.inc(ins)
                        sr_v = self.sV.n
                        prow_free_v = sr_v

                        @blk.tensor
                        def _(t, sr_v=sr_v, ghm_v=ghm_v):
                            t.wait_ge(self.sV.h, sr_v)
                            t.wait_ge(self.sV.h, ghm_v)
                            ins = t.matmul(PARG[:, 0:128], HLHS[0:1, :], SRB[:, :],
                                           start=True, stop=True)
                            self.sT.inc(ins)
                        g0_t = self.sT.n

                        @blk.scalar
                        def _(s, g0_t=g0_t):
                            s.wait_ge(self.sT.h, g0_t)
                            ins = s.activation(G0B[:, :], PARG[:, 0:128], AF.Exp)
                            self.sS.inc(ins)
                        g0_s = self.sS.n

                        @blk.vector
                        def _(v, chsl=chsl, g0_s=g0_s):
                            v.wait_ge(self.sS.h, g0_s)
                            ins = v.tensor_tensor(G0B[:, :], G0B[:, :], CC[:, chsl],
                                                  ALU.mult)
                            self.sV.inc(ins)
                        g0_v = self.sV.n
                        parg_free = (g0_s, self.sV.n)

                        pl_marks = []
                        for l in range(8):
                            pp = PP1 if l % 2 == 0 else PP2

                            @blk.tensor
                            def _(t, l=l, pp=pp, ghm_v=ghm_v):
                                t.wait_ge(self.sV.h, ghm_v)
                                if len(pl_marks) >= 2:
                                    t.wait_ge(self.sV.h, pl_marks[-2])
                                ins = t.matmul(pp[:, :], HBUF[:, l, :], GBUF[:, l, :],
                                               start=True, stop=True)
                                self.sT.inc(ins)
                            pl_t = self.sT.n

                            @blk.vector
                            def _(v, l=l, pp=pp, pl_t=pl_t):
                                v.wait_ge(self.sT.h, pl_t)
                                if l == 0:
                                    ins = v.tensor_tensor(MS[:, :], pp[:, :],
                                                          LVLB[:, 0:128], ALU.mult)
                                else:
                                    v.tensor_tensor(TMB[:, :], pp[:, :],
                                                    LVLB[:, l * 128:(l + 1) * 128],
                                                    ALU.mult)
                                    ins = v.tensor_tensor(MS[:, :], MS[:, :],
                                                          TMB[:, :], ALU.add)
                                self.sV.inc(ins)
                            pl_marks.append(self.sV.n)

                        @blk.vector
                        def _(v, m=pl_marks[-1]):
                            v.wait_ge(self.sV.h, m)
                            ins = v.tensor_copy(MB[:, :], MS[:, :])
                            self.sV.inc(ins)
                        mb_v = self.sV.n

                    for dch in range(NCH16):
                        q = dch % 2
                        pxct = PXCTa if q == 0 else PXCTb

                        @blk.tensor
                        def _(t, dch=dch, q=q, chsl=chsl, pxct=pxct,
                              w=xct_v[-2], w2=ptr_free_v):
                            t.wait_ge(self.sS.h, xc_done)
                            if pass2:
                                t.wait_ge(self.sV.h, mm["hb"])
                            t.wait_ge(self.sV.h, w)
                            if q == 0:
                                t.wait_ge(self.sV.h, w2)
                            ins = t.transpose(pxct[:, :],
                                              XC[:, dch, chsl], self.IDENT[:, :])
                            self.sT.inc(ins)
                        tr_t = self.sT.n

                        @blk.vector
                        def _(v, dch=dch, pxct=pxct, tr_t=tr_t):
                            v.wait_ge(self.sT.h, tr_t)
                            ins = v.tensor_copy(XCT[:, dch, :], pxct[:, :])
                            self.sV.inc(ins)
                        xct_v.append(self.sV.n)
                    xct_done = xct_v[-1]
                    del xct_v[:-2]

                    if pass2:
                        y_marks = [y_prev_v]
                        for dch in range(NCH16):
                            @blk.tensor
                            def _(t, dch=dch, m1=mb_v, m2=g0_v, m3=xct_done):
                                t.wait_ge(self.sV.h, max(m1, m2, m3))
                                t.wait_ge(self.sV.h, y_marks[-1])
                                t.matmul(PY[:, :], XCT[:, dch, :], MB[:, :],
                                         start=True, stop=False)
                                ins = t.matmul(PY[:, :],
                                               HB[:, dch * 128:(dch + 1) * 128],
                                               G0B[:, :], start=False, stop=True)
                                self.sT.inc(ins)
                            y_t = self.sT.n

                            @blk.vector
                            def _(v, dch=dch, chsl=chsl, y_t=y_t):
                                v.wait_ge(self.sT.h, y_t)
                                v.tensor_scalar_mul(TMB[:, :], XC[:, dch, chsl],
                                                    DVECB[:, dch:dch + 1])
                                ins = v.tensor_tensor(XC[:, dch, chsl], PY[:, :],
                                                      TMB[:, :], ALU.add)
                                self.sV.inc(ins)
                            y_marks.append(self.sV.n)

                    if (not pass2) and MSUB < 1:
                        continue
                    @blk.tensor
                    def _(t, scol=scol, pf=prow_free_v, pfs=prow_free_s):
                        t.wait_ge(self.sV.h, pf)
                        t.wait_ge(self.sS.h, pfs)
                        t.wait_ge(self.sV.h, srel_v)
                        ins = t.matmul(PROW[0:1, 0:128], scol, HENDDIFB[:, :],
                                       start=True, stop=True)
                        self.sT.inc(ins)
                    he1_t = self.sT.n

                    @blk.vector
                    def _(v, he1_t=he1_t):
                        v.wait_ge(self.sT.h, he1_t)
                        ins = v.tensor_copy(HR2[0:1, 0:128], PROW[0:1, 0:128])
                        self.sV.inc(ins)
                    he1_v = self.sV.n

                    @blk.tensor
                    def _(t, ch=ch, he1_v=he1_v):
                        t.wait_ge(self.sV.h, he1_v)
                        ins = t.matmul(PROW[0:1, 0:128], LNDTC[:, ch:ch + 1],
                                       IDF[:, :], start=True, stop=True)
                        self.sT.inc(ins)
                    he2_t = self.sT.n

                    @blk.vector
                    def _(v, he2_t=he2_t):
                        v.wait_ge(self.sT.h, he2_t)
                        ins = v.tensor_copy(TROW[0:1, 0:128], PROW[0:1, 0:128])
                        self.sV.inc(ins)
                    he2_v = self.sV.n
                    prow_free_v = he2_v

                    if (not pass2) and MSUB < 2:
                        continue

                    @blk.tensor
                    def _(t, he2_v=he2_v, pfs=parg_free[0], pfv=parg_free[1]):
                        t.wait_ge(self.sV.h, he2_v)
                        t.wait_ge(self.sV.h, arow_v)
                        t.wait_ge(self.sS.h, pfs)
                        t.wait_ge(self.sV.h, pfv)
                        t.matmul(PARG[:, 0:128], HLHS[0:1, :], HR2[0:1, 0:128],
                                 start=True, stop=False)
                        ins = t.matmul(PARG[:, 0:128], self.ONER128_F[0:1, 0:64],
                                       TROW[0:1, 0:128], start=False, stop=True)
                        self.sT.inc(ins)
                    hea_t = self.sT.n

                    @blk.scalar
                    def _(s, hea_t=hea_t):
                        s.wait_ge(self.sT.h, hea_t)
                        ins = s.activation(HBUF[:, 0, :], PARG[:, 0:128], AF.Exp)
                        self.sS.inc(ins)
                    hee_s = self.sS.n

                    @blk.vector
                    def _(v, chsl=chsl, hee_s=hee_s):
                        v.wait_ge(self.sS.h, hee_s)
                        ins = v.tensor_tensor(HBUF[:, 0, :], HBUF[:, 0, :],
                                              BB[:, chsl], ALU.mult)
                        self.sV.inc(ins)
                    hev_v = self.sV.n
                    parg_free = (hee_s, hev_v)

                    if (not pass2) and MSUB < 3:
                        continue

                    @blk.tensor
                    def _(t, hev_v=hev_v, w=xct_done):
                        t.wait_ge(self.sV.h, hev_v)
                        t.wait_ge(self.sV.h, w)
                        ins = t.transpose(PXCTa[:, 0:64], HBUF[:, 0, :],
                                          self.IDENT[0:64, 0:64])
                        self.sT.inc(ins)
                    het_t = self.sT.n

                    @blk.vector
                    def _(v, het_t=het_t):
                        v.wait_ge(self.sT.h, het_t)
                        ins = v.tensor_copy(HENDTS[:, :], PXCTa[:, 0:64])
                        self.sV.inc(ins)
                    het_v = self.sV.n
                    ptr_free_v = het_v

                    if (not pass2) and MSUB < 4:
                        continue
                    @blk.tensor
                    def _(t, ch=ch, w=prow_free_v):
                        t.wait_ge(self.sV.h, w)
                        ins = t.matmul(PROW[:, 256:257], HLHS[0:1, :],
                                       SENDR[0:1, ch:ch + 1], start=True, stop=True)
                        self.sT.inc(ins)
                    dc_t = self.sT.n

                    @blk.scalar
                    def _(s, dc_t=dc_t):
                        s.wait_ge(self.sT.h, dc_t)
                        ins = s.activation(DCOL[:, :], PROW[:, 256:257], AF.Exp)
                        self.sS.inc(ins)
                    dc_s = self.sS.n
                    prow_free_s = dc_s

                    if (not pass2) and MSUB < 6:
                        continue
                    for qd in range(4):
                        @blk.tensor
                        def _(t, qd=qd, het_v=het_v, xct_done=xct_done,
                              hupd_v=hupd_v):
                            t.wait_ge(self.sV.h, het_v)
                            t.wait_ge(self.sV.h, xct_done)
                            t.wait_ge(self.sV.h, hupd_v)
                            if pass2:
                                t.wait_ge(self.sV.h, mm["hb"])
                            ins = t.matmul(
                                PCC2[:, :], HENDTS[:, :],
                                XCT[:, qd * 4:(qd + 1) * 4, :].rearrange(
                                    "p a b -> p (a b)"),
                                start=True, stop=True)
                            self.sT.inc(ins)
                        cq_t = self.sT.n

                        @blk.vector
                        def _(v, qd=qd, cq_t=cq_t, dc_s=dc_s):
                            v.wait_ge(self.sT.h, cq_t)
                            v.wait_ge(self.sS.h, dc_s)
                            qsl = slice(qd * 512, (qd + 1) * 512)
                            v.tensor_scalar_mul(HST[:, qsl], HST[:, qsl], DCOL[:, :])
                            ins = v.tensor_tensor(HST[:, qsl], PCC2[:, :],
                                                  HST[:, qsl], ALU.add)
                            self.sV.inc(ins)
                        hupd_v = self.sV.n

            emit_pass(pass2=False)
            contrib_done = self.sV.n
            if MSTAGE <= 2.5:
                stM3.close()
                return

            @blk.vector
            def _(v):
                v.wait_ge(self.sV.h, contrib_done)
                ins = v.reduce_sum(STOT[:, :], SENDR[:, :], axis=AX.X)
                self.sV.inc(ins)
            stot_v = self.sV.n

            @blk.tensor
            def _(t, ss=self.sS.n):
                t.wait_ge(self.sV.h, stot_v)
                t.wait_ge(self.sS.h, ss)
                ins = t.matmul(PARG[:, 0:1], HLHS[0:1, :], STOT[:, :],
                               start=True, stop=True)
                self.sT.inc(ins)
            dk_t = self.sT.n

            @blk.scalar
            def _(s):
                s.wait_ge(self.sT.h, dk_t)
                ins = s.activation(DKC[:, :], PARG[:, 0:1], AF.Exp)
                self.sS.inc(ins)
            dk_s = self.sS.n

            @blk.sync
            def _(q):
                q.wait_ge(self.sV.h, contrib_done)
                q.wait_ge(self.sS.h, dk_s)
                self.dS.inc(q.dma_start(out=ag_in.ap()[:, 0:2048], in_=HST[:, :]), 16)
                self.dS.inc(q.dma_start(out=ag_in.ap()[:, 2048:2049], in_=DKC[:, :],
                                        allow_slow_non_contiguous=True), 16)
            ag_w = self.dS.n

            @blk.gpsimd
            def _(g):
                g.wait_ge(self.dS.h, ag_w)
                self.sC.inc(g.collective_compute(
                    "AllGather", ALU.bypass,
                    replica_groups=[list(range(NCORES))],
                    ins=[ag_in.ap().opt()], outs=[ag_out.ap().opt()]))
                g.wait_ge(self.sC.h, self.sC.n)
                for b in range(NCORES):
                    self.dG.inc(g.dma_start(
                        out=AGD[:, b:b + 1],
                        in_=ag_out.ap()[b * 64:(b + 1) * 64, 2048:2049],
                        allow_slow_non_contiguous=True), 16)
            agd_ld = self.dG.n

            @blk.vector
            def _(v):
                v.wait_ge(self.dG.h, agd_ld)
                v.wait_ge(self.dS.h, mconst_ld)
                v.memset(O8[:, :], 1.0)
                v.tensor_tensor(DMB[:, :], AGD[:, :], O8[:, :], ALU.subtract)
                v.tensor_tensor(DMB[:, :], DMB[:, :], CMASK[:, :], ALU.mult)
                v.tensor_tensor(DMB[:, :], DMB[:, :], O8[:, :], ALU.add)
                v.memset(PFB[:, 7:8], 1.0)
                for b in range(6, -1, -1):
                    v.tensor_tensor(PFB[:, b:b + 1], PFB[:, b + 1:b + 2],
                                    DMB[:, b + 1:b + 2], ALU.mult)
                ins = v.tensor_tensor(WCOL[:, :], PFB[:, :], CMASK[:, :], ALU.mult)
                self.sV.inc(ins)
            wcol_v = self.sV.n

            @blk.vector
            def _(v):
                v.wait_ge(self.sV.h, wcol_v)
                ins = v.memset(HST[:, :], 0.0)
                self.sV.inc(ins)
            hz_v = self.sV.n

            hi_v = hz_v
            for b in range(NCORES):
                p = b % 2

                @blk.gpsimd
                def _(g, b=b, p=p, hz=(hz_v if b < 2 else 0)):
                    g.wait_ge(self.sC.h, self.sC.n)
                    if b >= 2:
                        g.wait_ge(self.sV.h, mm.get(f"agc{b - 2}", 0))
                    self.dG.inc(g.dma_start(
                        out=AGC[:, p, :],
                        in_=ag_out.ap()[b * 64:(b + 1) * 64, 0:2048]), 16)
                agc_ld = self.dG.n

                @blk.vector
                def _(v, b=b, p=p, agc_ld=agc_ld, hi_v=hi_v):
                    v.wait_ge(self.dG.h, agc_ld)
                    v.wait_ge(self.sV.h, hi_v)
                    ins = v.scalar_tensor_tensor(HST[:, :], AGC[:, p, :],
                                                 WCOL[:, b:b + 1], HST[:, :],
                                                 ALU.mult, ALU.add)
                    self.sV.inc(ins)
                hi_v = self.sV.n
                mm[f"agc{b}"] = hi_v
            hinit_v = hi_v

            if MSTAGE <= 3:
                stM3.close()
                return
            emit_pass(pass2=True)
            y_done = self.sV.n
            stM3.close()

            if MSTAGE <= 4:
                return
            stM6 = ExitStack()
            YGS2 = self.sb("YGS2", [128, NDC, 512], BF16, stM6)
            ZS = self.sb("ZS", [128, 512], BF16, stM6)
            YGATE = self.sb("YGATE", [128, NCH16, 512], BF16, stM6)
            UM = self.sb("UM", [128, NDC, 512], BF16, stM6)
            WINZB = self.sb("WINZB", [128, 2, NDC, 128], BF16, stM6)
            WOUTB = self.sb("WOUTB", [128, 2, NCH16, 128], BF16, stM6)
            PZ1 = self.ps("PZ1", [128, 512], F32, stM6)
            PZ2 = self.ps("PZ2", [128, 512], F32, stM6)
            PB1z = self.ps("PB1z", [128, 512], F32, stM6)
            PB2z = self.ps("PB2z", [128, 512], F32, stM6)
            PSTAz = self.ps("PSTAz", [1, 512], F32, stM6)
            PSTBz = self.ps("PSTBz", [1, 512], F32, stM6)

            winz_free = [self.sT.n, self.sT.n]
            wout_free = [self.sT.n, self.sT.n]
            pz_free = [(self.sS.n, self.sV.n), (self.sS.n, self.sV.n)]
            zs_free_v = self.sV.n
            um_free_v = self.sV.n
            for tsl in range(4):
                xsl = slice(tsl * 512, (tsl + 1) * 512)
                bw = lambda eng, m=max(y_done, self.sV.n): eng.wait_ge(self.sV.h, m)
                yg_m = self.ln_plain(lambda c, xsl=xsl: self.X[:, c, xsl],
                                     lambda c: YGS2[:, c, :],
                                     self.EPS[0:1, 0:1], PSTAz, PSTBz, PB1z, PB2z,
                                     extra_wait=bw, src_is_bf=True)

                for e in range(16):
                    p = e % 2
                    pz = PZ1 if p == 0 else PZ2

                    @blk.gpsimd
                    def _(g, e=e, p=p, first=(tsl == 0 and e == 0), yd=y_done,
                          ts=self.sT.n, ss=self.sS.n):
                        if first:
                            g.wait_ge(self.sV.h, yd)
                            g.wait_ge(self.sT.h, ts)
                            g.wait_ge(self.sS.h, ss)
                        g.wait_ge(self.sT.h, winz_free[p])
                        for c in range(NDC):
                            self.dG.inc(g.dma_start(
                                out=WINZB[:, p, c, :],
                                in_=self.inp["winT"][c * 128:(c + 1) * 128,
                                                     2048 + e * 128:2048 + (e + 1) * 128]),
                                16)
                    wz_ld = self.dG.n

                    @blk.tensor
                    def _(t, e=e, p=p, pz=pz, wz_ld=wz_ld, yg_m=yg_m):
                        t.wait_ge(self.dG.h, wz_ld)
                        t.wait_ge(self.sV.h, yg_m)
                        t.wait_ge(self.sS.h, pz_free[p][0])
                        t.wait_ge(self.sV.h, pz_free[p][1])
                        for c in range(NDC):
                            ins = t.matmul(pz[:, :], WINZB[:, p, c, :], YGS2[:, c, :],
                                           start=(c == 0), stop=(c == NDC - 1))
                        self.sT.inc(ins)
                    winz_free[p] = self.sT.n
                    z_t = self.sT.n

                    @blk.scalar
                    def _(s, pz=pz, z_t=z_t, zf=zs_free_v):
                        s.wait_ge(self.sT.h, z_t)
                        s.wait_ge(self.sV.h, zf)
                        ins = s.activation(ZS[:, :], pz[:, :], AF.Silu)
                        self.sS.inc(ins)
                    z_s = self.sS.n

                    @blk.vector
                    def _(v, e=e, xsl=xsl, z_s=z_s):
                        v.wait_ge(self.sS.h, z_s)
                        ins = v.tensor_tensor(YGATE[:, e, :], XC[:, e, xsl],
                                              ZS[:, :], ALU.mult)
                        self.sV.inc(ins)
                    zs_free_v = self.sV.n
                    pz_free[p] = (z_s, self.sV.n)
                ygate_v = self.sV.n

                for do in range(NDC):
                    p = do % 2
                    pz = PZ1 if p == 0 else PZ2

                    @blk.gpsimd
                    def _(g, do=do, p=p):
                        g.wait_ge(self.sT.h, wout_free[p])
                        for e in range(NCH16):
                            self.dG.inc(g.dma_start(
                                out=WOUTB[:, p, e, :],
                                in_=self.inp["woutT"][e * 128:(e + 1) * 128,
                                                      do * 128:(do + 1) * 128]), 16)
                    wo_ld = self.dG.n

                    @blk.tensor
                    def _(t, do=do, p=p, pz=pz, wo_ld=wo_ld, ygate_v=ygate_v):
                        t.wait_ge(self.dG.h, wo_ld)
                        t.wait_ge(self.sV.h, ygate_v)
                        t.wait_ge(self.sS.h, pz_free[p][0])
                        t.wait_ge(self.sV.h, pz_free[p][1])
                        for e in range(NCH16):
                            ins = t.matmul(pz[:, :], WOUTB[:, p, e, :], YGATE[:, e, :],
                                           start=(e == 0), stop=(e == NCH16 - 1))
                        self.sT.inc(ins)
                    wout_free[p] = self.sT.n
                    o_t = self.sT.n

                    @blk.vector
                    def _(v, do=do, pz=pz, o_t=o_t, uf=um_free_v):
                        v.wait_ge(self.sT.h, o_t)
                        v.wait_ge(self.sV.h, uf)
                        ins = v.tensor_copy(UM[:, do, :], pz[:, :])
                        self.sV.inc(ins)
                    pz_free[p] = (self.sS.n, self.sV.n)
                um_mark = self.sV.n

                uw = lambda eng, m=um_mark: eng.wait_ge(self.sV.h, m)
                self.ln_residual(lambda c: UM[:, c, :], xsl,
                                 self.EPS[0:1, 0:1], PSTAz, PSTBz, PB1z, PB2z,
                                 extra_wait=uw, src_is_bf=True)
                um_free_v = self.sV.n

            if DEBUG_TAPS:
                self.dbg["m_xc"] = (lambda XC=XC: XC[:, :, :].rearrange("p a b -> p (a b)"),
                                    [128, 16 * T], BF16)
                self.emit_taps()
            stM6.close()

    def phase_l1ffn(self):
        blk = self.block
        st = ExitStack()
        with st:
            YL = self.sb("YL", [128, NDC, 512], BF16, st)
            SW = self.sb("SW", [128, NDC, 1024], BF16, st)
            UM2 = self.sb("UM2", [128, NDC, 512], BF16, st)
            SWF = self.sb("SWF", [128, NDC, 1024], F32, st)
            PL1 = self.ps("PL1", [128, 512], F32, st)
            PL2 = self.ps("PL2", [128, 512], F32, st)
            PB1l = self.ps("PB1l", [128, 512], F32, st)
            PB2l = self.ps("PB2l", [128, 512], F32, st)
            PSTAl = self.ps("PSTAl", [1, 512], F32, st)
            PSTBl = self.ps("PSTBl", [1, 512], F32, st)

            marks = dict(T=self.sT.n, V=self.sV.n, S=self.sS.n)

            @blk.sync
            def _(q):
                q.wait_ge(self.sT.h, marks["T"])
                q.wait_ge(self.sV.h, marks["V"])
                q.wait_ge(self.sS.h, marks["S"])
                for c in range(NDC):
                    self.dS.inc(q.dma_start(
                        out=SWF[:, c, :],
                        in_=self.inp["l1wT"][c * 128:(c + 1) * 128, :]), 16)
            swf_ld = self.dS.n

            @blk.scalar
            def _(s):
                s.wait_ge(self.dS.h, swf_ld)
                for c in range(NDC):
                    ins = s.sign(SW[:, c, :], SWF[:, c, :])
                self.sS.inc(ins)
            sw_s = self.sS.n

            pl_free = [(marks["S"], marks["V"]), (marks["S"], marks["V"])]
            um_free_v = marks["V"]
            for tsl in range(4):
                xsl = slice(tsl * 512, (tsl + 1) * 512)
                bw = lambda eng, m=self.sV.n: eng.wait_ge(self.sV.h, m)
                yl_m = self.ln_plain(lambda c, xsl=xsl: self.X[:, c, xsl],
                                     lambda c: YL[:, c, :],
                                     self.EPS[0:1, 0:1], PSTAl, PSTBl, PB1l, PB2l,
                                     extra_wait=bw, src_is_bf=True)

                for do in range(NDC):
                    p = do % 2
                    pl = PL1 if p == 0 else PL2

                    @blk.tensor
                    def _(t, do=do, pl=pl, yl_m=yl_m, sw_s=sw_s, p=p):
                        t.wait_ge(self.sS.h, sw_s)
                        t.wait_ge(self.sV.h, yl_m)
                        t.wait_ge(self.sS.h, pl_free[p][0])
                        t.wait_ge(self.sV.h, pl_free[p][1])
                        for c in range(NDC):
                            ins = t.matmul(pl[:, :], SW[:, c, do * 128:(do + 1) * 128],
                                           YL[:, c, :],
                                           start=(c == 0), stop=(c == NDC - 1))
                        self.sT.inc(ins)
                    l_t = self.sT.n

                    @blk.vector
                    def _(v, do=do, pl=pl, l_t=l_t, uf=um_free_v):
                        v.wait_ge(self.sT.h, l_t)
                        v.wait_ge(self.sV.h, uf)
                        ins = v.tensor_copy(UM2[:, do, :], pl[:, :])
                        self.sV.inc(ins)
                    pl_free[p] = (self.sS.n, self.sV.n)
                um_mark = self.sV.n

                uw = lambda eng, m=um_mark: eng.wait_ge(self.sV.h, m)
                self.ln_residual(lambda c: UM2[:, c, :], xsl,
                                 self.EPS[0:1, 0:1], PSTAl, PSTBl, PB1l, PB2l,
                                 extra_wait=uw, src_is_bf=True)
                um_free_v = self.sV.n


def _prep_inputs(
    L0_attn_x, L0_attn_w_qkv, L0_attn_b_qkv, L0_attn_w_fc, L0_attn_b_fc,
    L0_attn_gamma, L0_attn_beta,
    L0_ffn_w1_w, L0_ffn_w1_s, L0_ffn_w2_w, L0_ffn_w2_s,
    L0_ffn_h1_w, L0_ffn_h1_s, L0_ffn_h2_w, L0_ffn_h2_s,
    L0_ffn_w3_w, L0_ffn_w3_s,
    L1_attn_W_in, L1_attn_conv_w, L1_attn_conv_b, L1_attn_W_x,
    L1_attn_A_log, L1_attn_D, L1_attn_W_out, L1_ffn_weight,
):
    f = lambda a: np.ascontiguousarray(np.asarray(a), dtype=np.float32)
    col = lambda a: f(a).reshape(-1, 1)
    row = lambda a: f(a).reshape(1, -1)

    shared = {
        "identf": np.eye(128, dtype=np.float32),
        "wqkv": f(L0_attn_w_qkv), "bqkv": row(L0_attn_b_qkv),
        "wfc": f(L0_attn_w_fc), "bfc": row(L0_attn_b_fc),
        "gamcol": col(L0_attn_gamma), "betcol": col(L0_attn_beta),
        "w1T": f(L0_ffn_w1_w.T), "w2T": f(L0_ffn_w2_w.T),
        "h1T": f(L0_ffn_h1_w.T), "h2T": f(L0_ffn_h2_w.T),
        "w3T": f(L0_ffn_w3_w.T),
        "s1": np.ascontiguousarray(f(L0_ffn_w1_s).reshape(32, 128).T),
        "s2": np.ascontiguousarray(f(L0_ffn_w2_s).reshape(32, 128).T),
        "hs1": np.ascontiguousarray(f(L0_ffn_h1_s).reshape(32, 128).T),
        "hs2": np.ascontiguousarray(f(L0_ffn_h2_s).reshape(32, 128).T),
        "s3": np.ascontiguousarray(f(L0_ffn_w3_s).reshape(8, 128).T),
        "winT": f(L1_attn_W_in.T),
        "convw": np.ascontiguousarray(
            f(np.asarray(L1_attn_conv_w)[:, 0, :]).reshape(16, 128, 4)
            .transpose(1, 0, 2).reshape(128, 64)),
        "convb": np.ascontiguousarray(f(L1_attn_conv_b).reshape(16, 128).T),
        "wxT": f(L1_attn_W_x.T),
        "alog": col(L1_attn_A_log),
        "dvec": np.ascontiguousarray(f(L1_attn_D).reshape(16, 128).T),
        "woutT": f(L1_attn_W_out.T),
        "l1wT": f(L1_ffn_weight.T),
        "lvlmask": _level_masks(),
        "gdif": _gdif(),
        "tile8": np.ascontiguousarray(
            np.tile(np.eye(128, dtype=np.float32), (1, 8))),
        "ut": np.triu(np.ones((128, 128), dtype=np.float32)),
        "sel127": np.eye(128, dtype=np.float32)[:, 127:128].copy(),
        "henddif": np.ascontiguousarray(
            np.tile(np.eye(128, dtype=np.float32)[:, 127:128], (1, 128))
            - np.eye(128, dtype=np.float32)),
    }

    xT = np.ascontiguousarray(f(L0_attn_x)[0].T)
    in_maps = []
    for c in range(NCORES):
        m = dict(shared)
        m["xT"] = np.ascontiguousarray(xT[:, c * T:(c + 1) * T])
        cm = np.zeros((DS, 8), dtype=np.float32)
        cm[:, :c] = 1.0
        m["cmask"] = cm
        hm = np.zeros((128, 8), dtype=np.float32)
        if c > 0:
            hm[:, c - 1] = 1.0
        m["hmask"] = hm
        in_maps.append(m)
    return in_maps


def _level_masks():
    t = np.arange(128)
    tp = t[:, None]
    tt = t[None, :]
    masks = np.zeros((8, 128, 128), dtype=np.float32)
    for i, m in enumerate([64, 32, 16, 8, 4, 2, 1]):
        sel = (tt // m == tp // m + 1) & (tt // (2 * m) == tp // (2 * m))
        masks[i] = sel.astype(np.float32)
    masks[7] = np.eye(128, dtype=np.float32)
    return np.ascontiguousarray(masks.transpose(1, 0, 2).reshape(128, 8 * 128))


def _gdif():
    t = np.arange(128)
    eye = np.eye(128, dtype=np.float32)
    out = np.zeros((128, 8 * 128), dtype=np.float32)
    for l in range(7):
        m = 64 >> l
        bnd = (t // (2 * m)) * (2 * m) + m
        sel = np.zeros((128, 128), dtype=np.float32)
        sel[bnd - 1, t] = 1.0
        out[:, l * 128:(l + 1) * 128] = eye - sel
    out[:, 7 * 128:8 * 128] = 0.0
    return np.ascontiguousarray(out)


_CACHED = {}


def kernel(**inputs):
    if "nc" not in _CACHED:
        mk = MK()
        _CACHED["nc"] = mk.build()
        _CACHED["mk"] = mk
    nc = _CACHED["nc"]
    in_maps = _prep_inputs(**inputs)
    res = run_bass_kernel_spmd(nc, in_maps, core_ids=list(range(NCORES)))
    outs = [r["xo"] for r in res.results]
    full = np.concatenate(outs, axis=1)
    _CACHED["last_results"] = res
    return np.ascontiguousarray(full.T).reshape(1, NCORES * T, D)


if __name__ == "__main__":
    import reference as ref
    ins = ref.setup_inputs()
    ins = {k: np.asarray(v) for k, v in ins.items()}
    got = kernel(**ins)
    x = ins["L0_attn_x"]
    err = np.abs(got - x).max()
    print("identity check (N_BLOCKS=0 path):", err)
